# version 03
# baseline (speedup 1.0000x reference)
import math
import numpy as np
from contextlib import ExitStack

import concourse.bass as bass
import concourse.mybir as mybir
from concourse.bass_utils import run_bass_kernel_spmd

F32 = mybir.dt.float32
F32R = mybir.dt.float32r
BF16 = mybir.dt.bfloat16
U32 = mybir.dt.uint32
I32 = mybir.dt.int32
AF = mybir.ActivationFunctionType
ALU = mybir.AluOpType
AX = mybir.AxisListType

D = 1024
S = 4096
NB = 4
T_OWN = 2048
NTILE = T_OWN // 128


ENGINES = ("sync", "vector", "scalar", "tensor", "gpsimd")

class Prog:
    def __init__(self, nc):
        self.nc = nc
        self.q = {e: [] for e in ENGINES}
        self.cnt = {}
        self.last_wait = {e: {} for e in ENGINES}

    def sem(self, name):
        if name not in self.cnt:
            self.cnt[name] = 0
        return name

    def tok(self, name):
        return (name, self.cnt[name])

    def _norm_after(self, after):
        if after is None:
            return []
        if isinstance(after, (str, tuple)) and not (isinstance(after, tuple) and isinstance(after[0], (str,))) :
            after = [after]
        if isinstance(after, tuple) and isinstance(after[0], str):
            after = [after]
        out = []
        for a in after:
            if isinstance(a, str):
                out.append((a, self.cnt[a]))
            else:
                out.append(a)
        return out

    def emit(self, engine, fn, after=None, inc=None, k=1):
        waits = self._norm_after(after)
        incs = None
        if inc is not None:
            self.sem(inc)
            self.cnt[inc] += k
            incs = (inc, k)
        lw = self.last_wait[engine]
        fwaits = []
        for (s_, v) in waits:
            if v <= 0:
                continue
            if lw.get(s_, -1) >= v:
                continue
            lw[s_] = v
            fwaits.append((s_, v))
        self.q[engine].append((fwaits, fn, incs))
        return self.tok(inc) if inc is not None else None

    def t(self, fn, after=None, inc=None, k=1):
        return self.emit("tensor", fn, after, inc, k)
    def v(self, fn, after=None, inc=None, k=1):
        return self.emit("vector", fn, after, inc, k)
    def s(self, fn, after=None, inc=None, k=1):
        return self.emit("scalar", fn, after, inc, k)
    def g(self, fn, after=None, inc=None, k=1):
        return self.emit("gpsimd", fn, after, inc, k)
    def y(self, fn, after=None, inc=None, k=1):
        return self.emit("sync", fn, after, inc, k)

    def barrier(self, name):
        self.sem(name)
        base = self.cnt[name]
        for e in ENGINES:
            self.cnt[name] += 1
            self.q[e].append(([], lambda eng: eng.nop(), (name, 1)))
        tgt = base + len(ENGINES)
        for e in ENGINES:
            lw = self.last_wait[e]
            if lw.get(name, -1) < tgt:
                lw[name] = tgt
                self.q[e].append(([(name, tgt)], None, None))

    def build(self, block, sems):
        def mk(engine):
            def body(eng):
                for waits, fn, incs in self.q[engine]:
                    for (s_, v) in waits:
                        eng.wait_ge(sems[s_], v)
                    if fn is None:
                        continue
                    r = fn(eng)
                    if incs is not None:
                        s_, k = incs
                        r.then_inc(sems[s_], k)
            return body
        for e in ENGINES:
            if e == "sync":
                block.sync(mk(e))
            elif e == "vector":
                block.vector(mk(e))
            elif e == "scalar":
                block.scalar(mk(e))
            elif e == "tensor":
                block.tensor(mk(e))
            elif e == "gpsimd":
                block.gpsimd(mk(e))


def build_prog(nc, emitters, sbufs, psums, n_sems_extra=()):
    with ExitStack() as es:
        bufs = {}
        for (name, shape, dt) in sbufs:
            bufs[name] = es.enter_context(nc.sbuf_tensor("sb_" + name, shape, dt))
        for (name, shape, dt) in psums:
            bufs[name] = es.enter_context(nc.psum_tensor("ps_" + name, shape, dt))
        p = Prog(nc)
        for em in emitters:
            em(p, bufs)
        semnames = list(p.cnt.keys())
        sems = {n: es.enter_context(nc.semaphore(n)) for n in semnames}
        block = es.enter_context(nc.Block())
        p.build(block, sems)
    return nc



class PsumPool:
    def __init__(self, p, tensors):
        self.p = p
        self.ts = list(tensors)
        self.pend = [None] * len(self.ts)
        self.i = 0

    def get(self):
        j = self.i % len(self.ts)
        self.i += 1
        guard = self.pend[j]
        self.pend[j] = None
        return j, self.ts[j], ([guard] if guard else [])

    def release(self, j, tok):
        self.pend[j] = tok


def bfview(t, f32_off, n_bf16):
    assert n_bf16 % 2 == 0
    return t[:, f32_off:f32_off + n_bf16 // 2].bitcast(BF16)


class RotBuf:
    def __init__(self, n):
        self.n = n
        self.tok = [None] * n
        self.i = 0

    def get(self):
        j = self.i % self.n
        self.i += 1
        g = self.tok[j]
        self.tok[j] = None
        return j, ([g] if g else [])

    def release(self, j, tok):
        self.tok[j] = tok


QHEADS = [0, 4, 1, 5, 2, 6, 3, 7, 8, 12, 9, 13, 10, 14, 11, 15]

LN_EPS = 1e-5


def emit_ln(p, b, src_ap, dst_ap, after_toks, tinyslot, gamma=None,
            beta=None, eps_ap=None, resid=None):
    scr_ap = dst_ap
    """Token-major LayerNorm along free dim (D) of a [128, D] tile.
    scr_ap: [128, D] f32 scratch (distinct from src/dst). tinyslot: (tiny_ap,
    col0, guards) -- 6 consecutive [128,1] cols. Returns (done_tokens,
    tiny_release_token)."""
    tiny, c0, tguards = tinyslot
    ms = tiny[:, c0:c0 + 1]
    sq = tiny[:, c0 + 1:c0 + 2]
    mm_ = tiny[:, c0 + 2:c0 + 3]
    vv = tiny[:, c0 + 3:c0 + 4]
    rs = tiny[:, c0 + 4:c0 + 5]
    LNI = 1.0 / D
    a1 = p.v(lambda v: v.tensor_scalar(scr_ap, src_ap, 0.0, None, ALU.add, ALU.add, accum_out=ms),
             after=after_toks + tguards, inc="v")
    a2 = p.s(lambda s: s.activation(scr_ap, src_ap, AF.Square, accum_out=sq),
             after=after_toks + tguards + [a1], inc="s")
    a3 = p.v(lambda v: v.tensor_scalar(mm_, ms, LNI, None, ALU.mult), after=[a1], inc="v")
    a4 = p.s(lambda s: s.activation(vv, mm_, AF.Square), after=[a3], inc="s")
    a5 = p.v(lambda v: v.tensor_scalar(rs, sq, LNI, None, ALU.mult), after=[a2], inc="v")
    a6 = p.v(lambda v: v.tensor_tensor(vv, rs, vv, ALU.subtract), after=[a4, a5], inc="v")
    a7 = p.s(lambda s: s.activation(vv, vv, AF.Ln, bias=eps_ap), after=[a6], inc="s")
    a8 = p.s(lambda s: s.activation(rs, vv, AF.Exp, scale=-0.5), after=[a7], inc="s")
    a9 = p.v(lambda v: v.tensor_scalar(dst_ap, src_ap, mm_, rs, ALU.subtract, ALU.mult),
             after=[a8], inc="v")
    toks = [a9]
    if gamma is not None:
        a10 = p.v(lambda v: v.tensor_tensor(dst_ap, dst_ap, gamma, ALU.mult), after=toks, inc="v")
        a11 = p.v(lambda v: v.tensor_tensor(dst_ap, dst_ap, beta, ALU.add), after=[a10], inc="v")
        toks = [a11]
    if resid is not None:
        a12 = p.v(lambda v: v.tensor_tensor(dst_ap, dst_ap, resid, ALU.add), after=toks, inc="v")
        toks = [a12]
    return toks, a9



def emit_l0_attn(p, b, dram, upto='full'):
    nc = p.nc
    EPS = 1e-6
    fB, fC = b["fB"], b["fC"]
    xTb = bfview(fB, 0, 16384)
    xcb = bfview(fB, 8192, 16384)
    qfb = bfview(fC, 0, 16384)
    kfb = bfview(fC, 8192, 8192)
    kraw = bfview(fC, 8192, 8192)
    vraw = bfview(fC, 12288, 8192)
    vtm = bfview(fC, 12288, 8192)
    otm = bfview(fB, 0, 16384)
    oTb = bfview(fB, 8192, 16384)

    Wq_b = b["Wq_b"]; Wo_b = Wq_b
    Wk_b, Wv_b, Wc_b, We_b = b["Wk_b"], b["Wv_b"], b["Wc_b"], b["We_b"]
    bq8, bk2, bv2 = b["bq8"], b["bk2"], b["bv2"]
    bo_rep, l0g, l0b = b["bo_rep"], b["l0g_rep"], b["l0b_rep"]
    gate, triu, eyeb = b["gate"], b["triu"], b["eyeb"]
    eye64 = b["eye64"]
    ones_bf = b["ones_bf"]
    kcT = b["kcT"]
    Sbase, Sown = b["Sbase"], b["Sown"]
    kfT = b["kfT"]
    ams = b["ams"]
    scq = b["scq"]
    xt_f, yt_f, ln_f = b["xt_f"], b["yt_f"], b["ln_f"]
    tiny = b["tiny"]
    pool = PsumPool(p, [b[f"pp{i}"] for i in range(4)])
    psKV = [b[f"psk{i}"] for i in range(4)]
    scrot = RotBuf(2)
    tirot = RotBuf(4)

    ld = "a_ld"
    ldg = "a_ldg"
    p.g(lambda g: g.dma_start(out=xTb[:, :].rearrange("p (c t) -> p c t", c=8), in_=dram["xT"].rearrange("(c p) t -> p c t", p=128)), inc=ldg, k=16)
    p.g(lambda g: g.dma_start(out=xcb[:, :].rearrange("p (c t) -> p c t", c=8), in_=dram["xctxT"].rearrange("(c p) t -> p c t", p=128)), inc=ldg, k=16)
    p.g(lambda g: g.dma_start(out=Wq_b[:, :].rearrange("p (c m) -> p c m", c=8), in_=dram["Wq"].rearrange("(c p) m -> p c m", p=128)), inc=ldg, k=16)
    p.g(lambda g: g.dma_start(out=Wk_b[:, :].rearrange("p (c m) -> p c m", c=8), in_=dram["Wk"].rearrange("(c p) m -> p c m", p=128)), inc=ldg, k=16)
    p.g(lambda g: g.dma_start(out=Wv_b[:, :].rearrange("p (c m) -> p c m", c=8), in_=dram["Wv"].rearrange("(c p) m -> p c m", p=128)), inc=ldg, k=16)
    p.g(lambda g: g.dma_start(out=Wc_b[:, :].rearrange("p (c m) -> p c m", c=2), in_=dram["Wc"].rearrange("(c p) m -> p c m", p=128)), inc=ldg, k=16)
    p.g(lambda g: g.dma_start(out=We_b[:, :], in_=dram["We"]), inc=ldg, k=16)
    p.g(lambda g: g.dma_start(out=eyeb[:, :], in_=dram["eye"]), inc=ldg, k=16)
    p.g(lambda g: g.dma_start(out=b["eye64"][:, :], in_=dram["eye64"]), inc=ldg, k=16)
    for nm, dst in [("bq8", bq8), ("bk2", bk2), ("bv2", bv2), ("bo_rep", bo_rep),
                    ("l0g_rep", l0g), ("l0b_rep", l0b), ("gate", gate), ("triu", triu)]:
        src = dram[nm]
        p.y(lambda y, s_=src, d_=dst: y.dma_start(out=d_[:, :], in_=s_), inc=ld, k=16)
    p.v(lambda v: v.memset(ones_bf[:, :], 1.0), inc="v")
    epsc = b["tiny"][:, 70:71]
    p.v(lambda v: v.memset(epsc, LN_EPS), inc="v")
    ld_tok = p.tok(ld)
    ldg_tok = p.tok(ldg)
    ones_tok = p.tok("v")

    FEAT_DBG = [0]

    def feat_evac(ps, bias_ap, dst_ap, mm_tok, dbg=0):
        js, guards = scrot.get()
        o = 1536 * js
        t_min = scq[:, o:o + 512]
        t_exp = scq[:, o + 512:o + 1024]
        t_rel = scq[:, o + 1024:o + 1536]
        if bias_ap is not None:
            s1_ = p.s(lambda s: s.activation(t_rel, ps[:, 0:512], AF.Relu, bias=bias_ap),
                      after=[mm_tok] + guards, inc="s")
            v1 = p.v(lambda v: v.tensor_scalar(t_min, ps[:, 0:512], bias_ap, 0.0, ALU.add, ALU.min),
                     after=[mm_tok, s1_] + guards, inc="v")
        else:
            s1_ = p.s(lambda s: s.activation(t_rel, ps[:, 0:512], AF.Relu),
                      after=[mm_tok] + guards, inc="s")
            v1 = p.v(lambda v: v.tensor_scalar(t_min, ps[:, 0:512], 0.0, 0.0, ALU.add, ALU.min),
                     after=[mm_tok, s1_] + guards, inc="v")
        if dbg == 5:
            v2 = p.s(lambda s: s.copy(dst_ap, t_rel), after=[s1_], inc="s")
            scrot.release(js, v2)
            return v2
        if dbg == 6:
            v2 = p.s(lambda s: s.copy(dst_ap, t_min), after=[v1], inc="s")
            scrot.release(js, v2)
            return v2
        if dbg == 1:
            v2 = p.s(lambda s: s.copy(dst_ap, t_rel), after=[s1_, v1], inc="s")
            scrot.release(js, v2)
            return v2
        if dbg == 2:
            v2 = p.s(lambda s: s.copy(dst_ap, t_min), after=[s1_, v1], inc="s")
            scrot.release(js, v2)
            return v2
        s2_ = p.s(lambda s: s.activation(t_exp, t_min, AF.Exp), after=[v1], inc="s")
        if dbg == 3:
            v2 = p.s(lambda s: s.copy(dst_ap, t_exp), after=[s1_, s2_], inc="s")
            scrot.release(js, v2)
            return v2
        v2 = p.v(lambda v: v.tensor_tensor(dst_ap, t_exp, t_rel, ALU.add),
                 after=[s1_, s2_], inc="v")
        scrot.release(js, v2)
        return v2

    for mc in range(8):
        for nt in range(4):
            j, ps, guard = pool.get()
            for kc in range(8):
                p.t(lambda t, ps=ps, kc=kc, mc=mc, nt=nt: t.matmul(
                    ps[:, 0:512],
                    Wq_b[:, 1024 * kc + 128 * mc: 1024 * kc + 128 * (mc + 1)],
                    xTb[:, 2048 * kc + 512 * nt: 2048 * kc + 512 * (nt + 1)],
                    start=(kc == 0), stop=(kc == 7)),
                    after=([ld_tok, ldg_tok] if kc == 0 else []) + (guard if kc == 0 else []),
                    inc=("t" if kc == 7 else None))
            v2 = feat_evac(ps, bq8[:, mc:mc + 1],
                           qfb[:, 2048 * mc + 512 * nt: 2048 * mc + 512 * (nt + 1)], p.tok("t"))
            pool.release(j, v2)
    qf_ready = p.tok("v")

    def dump_and_stop(src_view, after_toks, a=16):
        rows = 128 * a
        p.g(lambda g: g.dma_start(
            out=dram["s1_out"][0:rows, :].rearrange("(a p) d -> p a d", p=128),
            in_=src_view.rearrange("p (a d) -> p a d", a=a)),
            after=after_toks, inc="a_dump", k=16)
        p.g(lambda g: g.nop(), after=["a_dump"], inc="a_fin")

    if upto == "qf":
        dump_and_stop(qfb, [qf_ready])
        return
    p.g(lambda g: g.dma_start(out=Wo_b[:, :].rearrange("p (c m) -> p c m", c=8), in_=dram["Wo"].rearrange("(c p) m -> p c m", p=128)),
        after=[qf_ready], inc=ldg, k=16)
    wo_tok = p.tok(ldg)

    for (W_b, bias2, dstraw) in [(Wk_b, bk2, kraw), (Wv_b, bv2, vraw)]:
        for mc in range(2):
            for half, xsrc in [(0, xcb), (1, xTb)]:
                for nt in range(4):
                    j, ps, guard = pool.get()
                    for kc in range(8):
                        p.t(lambda t, ps=ps, W_b=W_b, kc=kc, mc=mc, xsrc=xsrc, nt=nt: t.matmul(
                            ps[:, 0:512],
                            W_b[:, 256 * kc + 128 * mc: 256 * kc + 128 * (mc + 1)],
                            xsrc[:, 2048 * kc + 512 * nt: 2048 * kc + 512 * (nt + 1)],
                            start=(kc == 0), stop=(kc == 7)),
                            after=(guard if kc == 0 else None),
                            inc=("t" if kc == 7 else None))
                    mm_tok = p.tok("t")
                    off = 4096 * mc + 2048 * half + 512 * nt
                    sv = p.s(lambda s, ps=ps, b2=bias2, mc=mc, off=off, dr=dstraw: s.activation(
                        dr[:, off:off + 512], ps[:, 0:512], AF.Identity, bias=b2[:, mc:mc + 1]),
                        after=[mm_tok], inc="s")
                    pool.release(j, sv)

    if upto == "kraw":
        dump_and_stop(kraw, [p.tok("s")], a=8)
        return

    for which, raw in ([(0, kraw)] if upto in ("ke", "ke1", "kec") else [(0, kraw), (1, vraw)]):
        raw_done = p.tok("s")
        war = p.tok("t")
        for nt in range(8):
            j, ps, guard = pool.get()
            for kc in range(2):
                p.t(lambda t, ps=ps, kc=kc, raw=raw, nt=nt: t.matmul(
                    ps[0:64, 0:512],
                    Wc_b[:, 64 * kc: 64 * (kc + 1)],
                    raw[:, 4096 * kc + 512 * nt: 4096 * kc + 512 * (nt + 1)],
                    start=(kc == 0), stop=(kc == 1)),
                    after=([raw_done] + guard) if kc == 0 else None,
                    inc=("t" if kc == 1 else None))
            mm_tok = p.tok("t")
            sv = p.s(lambda s, ps=ps, nt=nt: s.copy(kcT[:, 512 * nt:512 * (nt + 1)], ps[0:64, 0:512]),
                     after=[mm_tok, war], inc="s")
            pool.release(j, sv)
        kc_done = p.tok("s")
        if upto == "kcT" and which == 0:
            p.g(lambda g: g.dma_start(
                out=dram["s1_out"][0:256, :].rearrange("(a p) d -> p a d", p=64),
                in_=kcT[:, :].rearrange("p (a d) -> p a d", a=4)),
                after=[kc_done], inc="a_dump", k=16)
            p.g(lambda g: g.nop(), after=["a_dump"], inc="a_fin")
            return
        if which == 0:
            kelist = [(0, 0)] if upto == "ke1" else [(mc, nt) for mc in range(2) for nt in range(8)]
            for mc, nt in kelist:
                    j, ps, guard = pool.get()
                    mmt = p.t(lambda t, ps=ps, mc=mc, nt=nt: t.matmul(
                        ps[:, 0:512],
                        We_b[:, 128 * mc:128 * (mc + 1)],
                        kcT[:, 512 * nt:512 * (nt + 1)],
                        start=True, stop=True),
                        after=[kc_done] + guard, inc="t")
                    if upto == "kec":
                        v2 = p.s(lambda s, ps=ps, mc=mc, nt=nt: s.copy(
                            kfb[:, 4096 * mc + 512 * nt: 4096 * mc + 512 * (nt + 1)], ps[:, 0:512]),
                            after=[mmt], inc="s")
                    else:
                        import os
                        v2 = feat_evac(ps, None,
                                       kfb[:, 4096 * mc + 512 * nt: 4096 * mc + 512 * (nt + 1)], mmt,
                                       dbg=int(os.environ.get("FEAT_DBG", "0")))
                    pool.release(j, v2)
        else:
            for tt in range(32):
                j, ps, guard = pool.get()
                mmt = p.t(lambda t, ps=ps, tt=tt: t.matmul(
                    ps[:, 0:256],
                    kcT[:, 128 * tt:128 * (tt + 1)],
                    We_b[:, :],
                    start=True, stop=True),
                    after=[kc_done] + guard, inc="t")
                sv = p.s(lambda s, ps=ps, tt=tt: s.copy(vtm[:, 256 * tt:256 * (tt + 1)], ps[:, 0:256]),
                         after=[mmt], inc="s")
                pool.release(j, sv)

    kv_ready = [p.tok("s"), p.tok("v")]
    if upto in ("ke", "ke1", "kec"):
        dump_and_stop(kfb, [p.tok("v"), p.tok("s")], a=8)
        return
    if upto == "kv":
        dump_and_stop(bfview(fC, 8192, 16384), kv_ready)
        return

    kfT_rot = RotBuf(2)

    def state_update(cc_global, start):
        for kv in range(4):
            blk = kv // 2
            base = 64 * (kv % 2)
            jf, fguards = kfT_rot.get()
            j, ps, guard = pool.get()
            trt = p.t(lambda t, ps=ps, kv=kv, cc=cc_global, base=base: t.transpose(
                ps[:, 0:32].bitcast(BF16)[:, 0:64],
                kfb[base:base + 64, 4096 * (kv // 2) + 128 * cc: 4096 * (kv // 2) + 128 * (cc + 1)],
                eye64[base:base + 64, :]),
                after=kv_ready + guard + fguards, inc="t")
            se = p.s(lambda s, ps=ps, jf=jf: s.copy(kfT[:, 64 * jf:64 * jf + 64],
                                                    ps[:, 0:32].bitcast(BF16)[:, 0:64]),
                     after=[trt], inc="s")
            pool.release(j, se)
            p.t(lambda t, kv=kv, cc=cc_global, start=start, jf=jf: t.matmul(
                psKV[kv][0:64, 0:64],
                kfT[:, 64 * jf:64 * jf + 64],
                vtm[:, 256 * cc + 64 * kv: 256 * cc + 64 * (kv + 1)],
                start=start, stop=False, skip_group_check=True),
                after=[se])
            m2 = p.t(lambda t, kv=kv, start=start, jf=jf: t.matmul(
                psKV[kv][0:64, 64:65],
                kfT[:, 64 * jf:64 * jf + 64],
                ones_bf[:, 0:1],
                start=False, stop=False, skip_group_check=True),
                after=[ones_tok], inc="t")
            kfT_rot.release(jf, m2)
        return p.tok("t")

    for cc in range(16):
        st_tok = state_update(cc, start=(cc == 0))
    for kv in range(4):
        base = 64 * (kv % 2)
        p.v(lambda v, kv=kv, base=base: v.tensor_scalar(
            Sbase[base:base + 64, 66 * (kv // 2): 66 * (kv // 2) + 65],
            psKV[kv][0:64, 0:65], gate[0:64, 0:1], None, ALU.mult),
            after=[st_tok], inc="v")
    gate_tok = p.tok("v")

    ams_i = [0]
    for cc in range(16):
        gcc = 16 + cc
        if cc > 0:
            sown_after = p.tok("t")
            for kv in range(4):
                base = 64 * (kv % 2)
                p.s(lambda s, kv=kv, base=base: s.copy(
                    Sown[base:base + 64, 66 * (kv // 2):66 * (kv // 2) + 65],
                    psKV[kv][0:64, 0:65]),
                    after=[sown_after], inc="s")
            sown_tok = p.tok("s")
        else:
            sown_tok = None
        for mc in range(8):
            for half in range(2):
                h = QHEADS[2 * mc + half]
                kv = h // 4
                base = 64 * half
                blk = kv // 2
                j, psA, guard = pool.get()
                mm1 = p.t(lambda t, ps=psA, base=base, kv=kv, gcc=gcc, mc=mc, cc=cc: t.matmul(
                    ps[:, 0:128],
                    kfb[base:base + 64, 4096 * (kv // 2) + 128 * gcc: 4096 * (kv // 2) + 128 * (gcc + 1)],
                    qfb[base:base + 64, 2048 * mc + 128 * cc: 2048 * mc + 128 * (cc + 1)],
                    start=True, stop=True),
                    after=kv_ready + [qf_ready] + guard,
                    inc="t")
                ai = ams_i[0] % 2
                ams_i[0] += 1
                amss = ams[:, 128 * ai:128 * ai + 128]
                vmask = p.v(lambda v, ps=psA, amss=amss: v.tensor_tensor(
                    amss, ps[:, 0:128], triu[:, :], ALU.mult),
                    after=[mm1], inc="v")
                pool.release(j, vmask)
                j2, psO, guard2 = pool.get()
                p.t(lambda t, ps=psO, amss=amss, gcc=gcc, kv=kv: t.matmul(
                    ps[:, 0:64], amss, vtm[:, 256 * gcc + 64 * kv:256 * gcc + 64 * (kv + 1)],
                    start=True, stop=False, skip_group_check=True),
                    after=[vmask] + guard2)
                p.t(lambda t, ps=psO, amss=amss: t.matmul(
                    ps[:, 64:65], amss, ones_bf[:, 0:1],
                    start=False, stop=False, skip_group_check=True),
                    after=[ones_tok])
                p.t(lambda t, ps=psO, base=base, mc=mc, cc=cc, blk=blk: t.matmul(
                    ps[:, 0:65],
                    qfb[base:base + 64, 2048 * mc + 128 * cc: 2048 * mc + 128 * (cc + 1)],
                    Sbase[base:base + 64, 66 * blk:66 * blk + 65],
                    start=False, stop=(cc == 0), skip_group_check=True),
                    after=[gate_tok], inc=("t" if cc == 0 else None))
                if cc > 0:
                    p.t(lambda t, ps=psO, base=base, mc=mc, cc=cc, blk=blk: t.matmul(
                        ps[:, 0:65],
                        qfb[base:base + 64, 2048 * mc + 128 * cc: 2048 * mc + 128 * (cc + 1)],
                        Sown[base:base + 64, 66 * blk:66 * blk + 65],
                        start=False, stop=True, skip_group_check=True),
                        after=[sown_tok], inc="t")
                o_tok = p.tok("t")
                jt, tguards = tirot.get()
                r1 = tiny[:, 32 + 2 * jt:33 + 2 * jt]
                r2 = tiny[:, 33 + 2 * jt:34 + 2 * jt]
                va = p.v(lambda v, ps=psO, r1=r1: v.tensor_scalar(r1, ps[:, 64:65], EPS, None, ALU.add),
                         after=[o_tok] + tguards, inc="v")
                vb = p.v(lambda v, r1=r1, r2=r2: v.reciprocal(r2, r1), after=[va], inc="v")
                vc_ = p.v(lambda v, ps=psO, cc=cc, h=h, r2=r2: v.tensor_scalar(
                    otm[:, 1024 * cc + 64 * h: 1024 * cc + 64 * (h + 1)],
                    ps[:, 0:64], r2, None, ALU.mult),
                    after=[vb], inc="v")
                pool.release(j2, vc_)
                tirot.release(jt, vc_)
        st_tok = state_update(gcc, start=False)
    attn_done = p.tok("v")
    if upto == "attn":
        dump_and_stop(otm, [attn_done])
        return

    for tt in range(16):
        for dc in range(8):
            j, ps, guard = pool.get()
            trt = p.t(lambda t, ps=ps, tt=tt, dc=dc: t.transpose(
                ps[:, 0:64].bitcast(BF16)[:, 0:128],
                otm[:, 1024 * tt + 128 * dc: 1024 * tt + 128 * (dc + 1)],
                eyeb[:, :]),
                after=[attn_done] + guard, inc="t")
            se = p.s(lambda s, ps=ps, tt=tt, dc=dc: s.copy(
                oTb[:, 2048 * dc + 128 * tt: 2048 * dc + 128 * (tt + 1)],
                ps[:, 0:64].bitcast(BF16)[:, 0:128]),
                after=[trt], inc="s")
            pool.release(j, se)
    oT_done = p.tok("s")

    xy_rot = RotBuf(2)
    xl_rot = RotBuf(1)
    for tt in range(16):
        ir, xyguards = xy_rot.get()
        _, xlguards = xl_rot.get()
        xt = xt_f[:, 0:1024]
        yt = yt_f[:, 1024 * ir:1024 * (ir + 1)]
        lt = ln_f[:, 0:1024]
        p.y(lambda y, tt=tt, xt=xt: y.dma_start(
            out=xt, in_=dram["x_tm"][128 * tt:128 * (tt + 1), :]),
            after=xyguards + xlguards, inc="a_xl", k=16)
        xl_tok = p.tok("a_xl")
        evs = []
        for dn in range(2):
            j, ps, guard = pool.get()
            for kc in range(8):
                p.t(lambda t, ps=ps, kc=kc, tt=tt, dn=dn: t.matmul(
                    ps[:, 0:512],
                    oTb[:, 2048 * kc + 128 * tt: 2048 * kc + 128 * (tt + 1)],
                    Wo_b[:, 1024 * kc + 512 * dn: 1024 * kc + 512 * (dn + 1)],
                    start=(kc == 0), stop=(kc == 7)),
                    after=([oT_done, wo_tok] + guard) if kc == 0 else None,
                    inc=("t" if kc == 7 else None))
            mm_tok = p.tok("t")
            v1 = p.v(lambda v, ps=ps, dn=dn, yt=yt, xt=xt: v.scalar_tensor_tensor(
                yt[:, 512 * dn:512 * (dn + 1)], ps[:, 0:512], 1.0, xt[:, 512 * dn:512 * (dn + 1)],
                ALU.mult, ALU.add),
                after=[mm_tok, xl_tok], inc="v")
            v2 = p.v(lambda v, dn=dn, yt=yt: v.tensor_tensor(
                yt[:, 512 * dn:512 * (dn + 1)], yt[:, 512 * dn:512 * (dn + 1)],
                bo_rep[:, 512 * dn:512 * (dn + 1)], ALU.add),
                after=[v1], inc="v")
            evs.append(v2)
            pool.release(j, v2)
        jt, tguards = tirot.get()
        t_inner, trel1 = emit_ln(p, b, yt[:, :], lt[:, :], evs + tguards,
                                 (tiny, 40 + 8 * jt, []), gamma=l0g[:, :], beta=l0b[:, :],
                                 eps_ap=epsc, resid=xt[:, :])
        s1t, trel2 = emit_ln(p, b, lt[:, :], yt[:, :], t_inner,
                             (tiny, 40 + 8 * jt, [trel1]), eps_ap=epsc)
        tirot.release(jt, trel2)
        xl_rot.release(0, s1t[0])
        stok = p.y(lambda y, tt=tt, yt=yt: y.dma_start(
            out=dram["s1_out"][128 * tt:128 * (tt + 1), :], in_=yt),
            after=s1t, inc=f"a_st{tt % 2}", k=16)
        xy_rot.release(ir, stok)
    p.y(lambda y: y.nop(), after=["a_st0", "a_st1"], inc="a_fin")



L0_SBUFS = [
    ("fB", [128, 16384], F32),
    ("fC", [128, 16384], F32),
    ("Wq_b", [128, 8192], BF16),
    ("Wk_b", [128, 2048], BF16), ("Wv_b", [128, 2048], BF16),
    ("Wc_b", [128, 128], BF16), ("We_b", [64, 256], BF16),
    ("bq8", [128, 8], F32), ("bk2", [128, 2], F32), ("bv2", [128, 2], F32),
    ("bo_rep", [128, 1024], F32), ("l0g_rep", [128, 1024], F32), ("l0b_rep", [128, 1024], F32),
    ("gate", [128, 1], F32), ("triu", [128, 128], F32), ("eyeb", [128, 128], BF16),
    ("eye64", [128, 64], BF16),
    ("ones_bf", [128, 2], BF16),
    ("kcT", [64, 4096], BF16),
    ("Sbase", [128, 132], BF16), ("Sown", [128, 132], BF16),
    ("kfT", [128, 128], BF16), ("ams", [128, 256], BF16),
    ("scq", [128, 3072], F32),
    ("xt_f", [128, 1024], F32), ("yt_f", [128, 2048], F32), ("ln_f", [128, 1024], F32),
    ("tiny", [128, 72], F32),
]
L0_PSUMS = [(f"pp{i}", [128, 512], F32) for i in range(4)] + [
    (f"psk{i}", [64, 65], F32) for i in range(4)]

L0_DRAM_INS = ["xT", "xctxT", "x_tm", "Wq", "Wk", "Wv", "Wc", "We", "Wo",
               "bq8", "bk2", "bv2", "bo_rep", "l0g_rep", "l0b_rep", "gate",
               "triu", "eye", "eye64"]
L0_DRAM_SHAPES = {
    "eye64": [128, 64],
    "xT": [D, T_OWN], "xctxT": [D, T_OWN], "x_tm": [T_OWN, D],
    "Wq": [D, D], "Wk": [D, 256], "Wv": [D, 256], "Wc": [256, 64], "We": [64, 256],
    "Wo": [D, D], "bq8": [128, 8], "bk2": [128, 2], "bv2": [128, 2],
    "bo_rep": [128, 1024], "l0g_rep": [128, 1024], "l0b_rep": [128, 1024],
    "gate": [128, 1], "triu": [128, 128], "eye": [128, 128],
}


def build_l0_standalone(upto="full"):
    nc = bass.Bass("TRN2", target_bir_lowering=False)
    dram = {}
    for nm in L0_DRAM_INS:
        dram[nm] = nc.dram_tensor(nm, L0_DRAM_SHAPES[nm], F32, kind="ExternalInput").ap()
    dram["s1_out"] = nc.dram_tensor("s1_out", [T_OWN, D], F32, kind="ExternalOutput").ap()
    return build_prog(nc, [lambda p, b: emit_l0_attn(p, b, dram, upto=upto)], L0_SBUFS, L0_PSUMS)


def stage_l0_inputs(core, inputs):
    b_, h_ = core // 2, core % 2
    x = inputs["L0_attn_x"]
    own = np.ascontiguousarray(x[b_, 2048 * h_:2048 * (h_ + 1)])
    ctx = np.ascontiguousarray(x[b_, 2048 * (1 - h_):2048 * (2 - h_)])
    Wq = inputs["L0_attn_Wq"]
    bq = inputs["L0_attn_bq"]
    perm = np.concatenate([np.arange(64 * h, 64 * h + 64) for h in QHEADS])
    Wq_p = np.ascontiguousarray(Wq[:, perm])
    bq_p = np.ascontiguousarray(bq[perm])
    d = {
        "xT": np.ascontiguousarray(own.T),
        "xctxT": np.ascontiguousarray(ctx.T),
        "x_tm": own,
        "Wq": Wq_p,
        "Wk": np.ascontiguousarray(inputs["L0_attn_Wk"]),
        "Wv": np.ascontiguousarray(inputs["L0_attn_Wv"]),
        "Wc": np.ascontiguousarray(inputs["L0_attn_Wc"]),
        "We": np.ascontiguousarray(inputs["L0_attn_We"]),
        "Wo": np.ascontiguousarray(inputs["L0_attn_Wo"]),
        "bq8": np.ascontiguousarray(bq_p.reshape(8, 128).T),
        "bk2": np.ascontiguousarray(inputs["L0_attn_bk"].reshape(2, 128).T),
        "bv2": np.ascontiguousarray(inputs["L0_attn_bv"].reshape(2, 128).T),
        "bo_rep": np.tile(inputs["L0_attn_bo"][None, :], (128, 1)),
        "l0g_rep": np.tile(inputs["L0_attn_gamma"][None, :], (128, 1)),
        "l0b_rep": np.tile(inputs["L0_attn_beta"][None, :], (128, 1)),
        "gate": np.full((128, 1), float(h_), np.float32),
        "triu": np.triu(np.ones((128, 128), np.float32)),
        "eye": np.eye(128, dtype=np.float32),
        "eye64": np.tile(np.eye(64, dtype=np.float32), (2, 1)),
    }
    return {k: np.ascontiguousarray(v.astype(np.float32)) for k, v in d.items()}



def emit_moe(p, b, dram, pre, inner_ln):
    E, HID = 8, 4096
    fA = b["fA"]
    s1T = b["m_s1T"]
    s1Tf = b["m_s1Tf"]
    hq = b["m_hq"]
    W1q = [b["m_w1a"]] + ([b["m_w1b"]] if "m_w1b" in b else [])
    W2q = b["m_w2"]
    wrep = b["m_wrep"]
    stile = b["m_st"]
    Wrs = b["m_wr"]
    brs = b["m_br"]
    b1r = b["m_b1r"]
    b2s = b["m_b2s"]
    sel8 = b["m_sel8"]
    ltile = b["m_lt"]
    tops = b["m_tops"]
    tidx = b["m_tidx"]
    wtm = b["m_wtm"]
    w8 = b["m_w8"]
    eyef = b["m_eye"]
    onesrow = b["m_ones1"]
    yt = b.get("m_yt")
    lt2 = b["m_s1Tf"]
    tiny = b["m_tiny"]
    lnw = b.get("m_lnw")
    lnb = b.get("m_lnb")
    pool = PsumPool(p, [b[f"mpp{i}"] for i in range(6)])

    ld = "m_ld"
    ldg = "m_ldg"

    p.y(lambda y: y.dma_start(out=Wrs[:, :].rearrange("p (c e) -> p c e", c=8),
                              in_=dram["Wr"].rearrange("(c p) e -> p c e", p=128)), inc=ld, k=16)
    p.y(lambda y: y.dma_start(out=brs[:, :], in_=dram["brs"]), inc=ld, k=16)
    p.y(lambda y: y.dma_start(out=b1r[:, :], in_=dram["b1r"]), inc=ld, k=16)
    p.g(lambda g: g.dma_start(out=b2s[:, :], in_=dram["b2s"]), inc=ldg, k=16)
    p.y(lambda y: y.dma_start(out=eyef[:, :], in_=dram["eye"]), inc=ld, k=16)
    if inner_ln:
        p.y(lambda y: y.dma_start(out=lnw[:, :], in_=dram["lnw_rep"]), inc=ld, k=16)
        p.y(lambda y: y.dma_start(out=lnb[:, :], in_=dram["lnb_rep"]), inc=ld, k=16)
    p.g(lambda g: g.dma_start(out=sel8[:, :], in_=dram["sel8"]), inc=ldg, k=16)

    p.v(lambda v: v.memset(onesrow[:, :], 1.0), inc="v")
    epsc = tiny[:, 64:65]
    p.v(lambda v: v.memset(epsc, LN_EPS), inc="v")
    eps12 = tiny[:, 65:66]
    p.v(lambda v: v.memset(eps12, 1e-12), inc="v")
    ld_tok, ldg_tok, ones_tok = p.tok(ld), p.tok(ldg), p.tok("v")

    st_rot = RotBuf(1)
    s1tf_rot = RotBuf(1)
    seed_toks = []
    for tt in range(NTILE):
        _, stg = st_rot.get()
        sx = p.y(lambda y, tt=tt: y.dma_start(out=stile[:, :], in_=dram["s_in"][128 * tt:128 * (tt + 1), :]),
                 after=stg, inc="m_sx", k=16)
        sx = p.tok("m_sx")
        if pre == "combine":
            yg_ = [p.tok("m_yy")] if p.cnt.get("m_yy") else []
            yx = p.y(lambda y, tt=tt: y.dma_start(out=yt[:, :], in_=dram["y_in"][128 * tt:128 * (tt + 1), :]),
                     after=yg_, inc="m_yx", k=16)
            yx = p.tok("m_yx")
            v0 = p.v(lambda v: v.tensor_tensor(yt[:, :], stile[:, :], yt[:, :], ALU.add),
                     after=[sx, yx], inc="v")
            if "y_in2" in dram:
                sxb = p.y(lambda y, tt=tt: y.dma_start(out=stile[:, :], in_=dram["y_in2"][128 * tt:128 * (tt + 1), :]),
                          after=[v0], inc="m_sx", k=16)
                sxb = p.tok("m_sx")
                v0 = p.v(lambda v: v.tensor_tensor(yt[:, :], stile[:, :], yt[:, :], ALU.add),
                         after=[sxb], inc="v")
            lnt, _rel = emit_ln(p, b, yt[:, :], stile[:, :], [v0, ones_tok], (tiny, 0, []),
                                eps_ap=epsc)
            tk = p.v(lambda v: v.tensor_copy(tops[:, :], tops[:, :]), after=lnt, inc="m_yy")
            sx = lnt[0]
            s_ready = lnt
        else:
            s_ready = [sx]
        _, s1g = s1tf_rot.get()
        trts = []
        for dc in range(8):
            j, ps, guard = pool.get()
            trt = p.t(lambda t, ps=ps, tt=tt, dc=dc: t.transpose(
                ps[:, 0:128], stile[:, 128 * dc:128 * (dc + 1)], eyef[:, :]),
                after=s_ready + guard + [ld_tok], inc="t")
            e1 = p.s(lambda s, ps=ps, dc=dc: s.copy(s1Tf[:, 128 * dc:128 * (dc + 1)], ps[:, 0:128]),
                     after=[trt] + s1g, inc="s")
            e2 = p.s(lambda s, ps=ps, tt=tt, dc=dc: s.copy(
                s1T[:, 2048 * dc + 128 * tt: 2048 * dc + 128 * (tt + 1)].bitcast(BF16)[:, 0:128]
                if False else s1T[:, 2048 * dc + 128 * tt: 2048 * dc + 128 * (tt + 1)],
                ps[:, 0:128]),
                after=[trt], inc="s")
            trts.append(p.tok("s"))
            pool.release(j, e2)
        j, psl, guard = pool.get()
        for dc in range(8):
            p.t(lambda t, ps=psl, dc=dc: t.matmul(
                ps[:, 0:8], s1Tf[:, 128 * dc:128 * (dc + 1)], Wrs[:, 8 * dc:8 * (dc + 1)],
                start=(dc == 0), stop=False, skip_group_check=True),
                after=(trts + guard) if dc == 0 else None)
        brmm = p.t(lambda t, ps=psl: t.matmul(
            ps[:, 0:8], onesrow[:, :], brs[:, :], start=False, stop=True, skip_group_check=True),
            after=[ones_tok], inc="t")
        s1tf_rot.release(0, brmm)
        le = p.s(lambda s, ps=psl: s.copy(ltile[:, :], ps[:, 0:8]), after=[brmm], inc="s")
        pool.release(j, le)
        p.v(lambda v: v.max_with_indices(tops[:, :], tidx[:, :], ltile[:, :]), after=[le])
        r1 = p.v(lambda v: v.tensor_copy(tops[:, :], tops[:, :]), inc="v")
        nm1 = tiny[:, 66:67]
        r2 = p.v(lambda v: v.tensor_scalar(nm1, tops[:, 0:1], -1.0, None, ALU.mult), after=[r1], inc="v")
        r4 = p.v(lambda v, tt=tt: v.tensor_scalar(wtm[:, 8 * tt:8 * (tt + 1)], ltile[:, :], tops[:, 1:2], None, ALU.is_ge),
                 after=[r1], inc="v")
        r3 = p.s(lambda s: s.activation(ltile[:, :], ltile[:, :], AF.Exp, bias=nm1), after=[r2, r4, le], inc="s")
        r5 = p.v(lambda v, tt=tt: v.tensor_tensor(wtm[:, 8 * tt:8 * (tt + 1)], ltile[:, :], wtm[:, 8 * tt:8 * (tt + 1)], ALU.mult),
                 after=[r3, r4], inc="v")
        dn = tiny[:, 67:68]
        rdn = tiny[:, 68:69]
        r6 = p.v(lambda v, tt=tt: v.tensor_scalar(ltile[:, :], wtm[:, 8 * tt:8 * (tt + 1)], 1.0, None,
                                                  ALU.mult, ALU.add, accum_out=dn),
                 after=[r5], inc="v")
        r7 = p.v(lambda v: v.reciprocal(rdn, dn), after=[r6], inc="v")
        r8 = p.v(lambda v, tt=tt: v.tensor_scalar(wtm[:, 8 * tt:8 * (tt + 1)], wtm[:, 8 * tt:8 * (tt + 1)], rdn, None, ALU.mult),
                 after=[r7], inc="v")
        j2, ps2, guard2 = pool.get()
        trw = p.t(lambda t, ps=ps2, tt=tt: t.transpose(ps[0:8, 0:128], wtm[:, 8 * tt:8 * (tt + 1)], eyef[:, :]),
                  after=[r8] + guard2, inc="t")
        we_ = p.s(lambda s, ps=ps2, tt=tt: s.copy(w8[:, 128 * tt:128 * (tt + 1)], ps[0:8, 0:128]),
                  after=[trw], inc="s")
        pool.release(j2, we_)
        for dh in range(2):
            j3, ps3, guard3 = pool.get()
            bmm = p.t(lambda t, ps=ps3, tt=tt, dh=dh: t.matmul(
                ps[:, 0:512], w8[:, 128 * tt:128 * (tt + 1)], b2s[:, 512 * dh:512 * (dh + 1)],
                start=True, stop=True),
                after=[we_, ldg_tok] + guard3, inc="t")
            sd = p.v(lambda v, ps=ps3, tt=tt, dh=dh: v.scalar_tensor_tensor(
                fA[:, 1024 * tt + 512 * dh: 1024 * tt + 512 * (dh + 1)],
                ps[:, 0:512], 1.0, stile[:, 512 * dh:512 * (dh + 1)], ALU.mult, ALU.add),
                after=[bmm] + s_ready, inc="v")
            seed_toks.append(sd)
            pool.release(j3, sd)
        st_rot.release(0, seed_toks[-1])
    w8_done = p.tok("s")
    seeds_done = p.tok("v")

    w1_sems = ["m_w1a_ld", "m_w1b_ld"][:len(W1q) if True else 2]
    w1_guard = [None, None]
    w2_guard = None
    hq_guard = None
    wrep_guard = None
    prev_w2_done = None
    for e in range(E):
        for c4 in range(4):
            jw, psw, guardw = pool.get()
            mmt = p.t(lambda t, ps=psw, e=e, c4=c4: t.matmul(
                ps[:, 0:512],
                sel8[:, 128 * e:128 * (e + 1)],
                w8[:, 512 * c4:512 * (c4 + 1)],
                start=True, stop=True),
                after=[w8_done] + guardw + ([wrep_guard] if wrep_guard else []),
                inc="t")
            ev_ = p.s(lambda s, ps=psw, c4=c4: s.copy(wrep[:, 512 * c4:512 * (c4 + 1)], ps[:, 0:512]),
                      after=[mmt], inc="s")
            pool.release(jw, ev_)
        wrep_tok = p.tok("s")
        for q in range(4):
            bsel = (e * 4 + q) % len(W1q)
            wsem = w1_sems[bsel]
            p.g(lambda g, e=e, q=q, bsel=bsel: g.dma_start(
                out=W1q[bsel][:, :].rearrange("p (c m) -> p c m", c=8),
                in_=dram["W1"][e].rearrange("(c p) m -> p c m", p=128)[:, :, 1024 * q:1024 * (q + 1)]),
                after=w1_guard[bsel], inc=wsem, k=16)
            w1_tok = p.tok(wsem)
            p.g(lambda g, e=e, q=q: g.dma_start(
                out=W2q[:, :].rearrange("p (c m) -> p c m", c=8),
                in_=dram["W2"][e][1024 * q:1024 * (q + 1), :].rearrange("(c p) m -> p c m", p=128)),
                after=([prev_w2_done] if prev_w2_done else None), inc="m_w2_ld", k=16)
            w2_tok = p.tok("m_w2_ld")
            w1_mm_last = None
            for mc in range(8):
                nts = []
                for nt in range(4):
                    j, ps, guard = pool.get()
                    for kc in range(8):
                        p.t(lambda t, ps=ps, mc=mc, nt=nt, kc=kc, bsel=bsel: t.matmul(
                            ps[:, 0:512],
                            W1q[bsel][:, 1024 * kc + 128 * mc: 1024 * kc + 128 * (mc + 1)],
                            s1T[:, 2048 * kc + 512 * nt: 2048 * kc + 512 * (nt + 1)],
                            start=(kc == 0), stop=(kc == 7)),
                            after=([w1_tok, seeds_done, w8_done] + guard + ([hq_guard] if hq_guard else [])) if kc == 0 else None,
                            inc=("t" if kc == 7 else None))
                    mm_tok = p.tok("t")
                    w1_mm_last = mm_tok
                    ge = p.s(lambda s, ps=ps, e=e, q=q, mc=mc, nt=nt: s.activation(
                        hq[:, 2048 * mc + 512 * nt: 2048 * mc + 512 * (nt + 1)],
                        ps[:, 0:512], AF.Gelu, bias=b1r[:, 32 * e + 8 * q + mc: 32 * e + 8 * q + mc + 1]),
                        after=[mm_tok], inc="s")
                    pool.release(j, ge)
                    nts.append(ge)
                p.v(lambda v, mc=mc: v.tensor_tensor(
                    hq[:, 2048 * mc:2048 * (mc + 1)], hq[:, 2048 * mc:2048 * (mc + 1)],
                    wrep[:, :], ALU.mult),
                    after=nts + [wrep_tok], inc="v")
            hs_done = p.tok("v")
            for tt in range(NTILE):
                for dh in range(2):
                    j, ps, guard = pool.get()
                    for kc in range(8):
                        p.t(lambda t, ps=ps, tt=tt, dh=dh, kc=kc: t.matmul(
                            ps[:, 0:512],
                            hq[:, 2048 * kc + 128 * tt: 2048 * kc + 128 * (tt + 1)],
                            W2q[:, 1024 * kc + 512 * dh: 1024 * kc + 512 * (dh + 1)],
                            start=(kc == 0), stop=(kc == 7)),
                            after=([hs_done, w2_tok] + guard) if kc == 0 else None,
                            inc=("t" if kc == 7 else None))
                    mm_tok = p.tok("t")
                    ad = p.v(lambda v, ps=ps, tt=tt, dh=dh: v.scalar_tensor_tensor(
                        fA[:, 1024 * tt + 512 * dh: 1024 * tt + 512 * (dh + 1)],
                        ps[:, 0:512], 1.0,
                        fA[:, 1024 * tt + 512 * dh: 1024 * tt + 512 * (dh + 1)],
                        ALU.mult, ALU.add),
                        after=[mm_tok], inc="v")
                    pool.release(j, ad)
            prev_w2_done = p.tok("t")
            w1_guard[bsel] = p.tok("t")
            hq_guard = prev_w2_done
        wrep_guard = p.tok("v")
    acc_done = p.tok("v")

    ti2 = RotBuf(2)
    st2 = RotBuf(1)
    lt_rot = RotBuf(1)
    for tt in range(NTILE):
        jt, tg = ti2.get()
        _, ltg = lt_rot.get()
        tg = tg + ltg
        acc_tile = fA[:, 1024 * tt:1024 * (tt + 1)]
        if inner_ln:
            _, sg = st2.get()
            sx2 = p.y(lambda y, tt=tt: y.dma_start(out=stile[:, :], in_=dram["s_in"][128 * tt:128 * (tt + 1), :]),
                      after=[acc_done] + sg, inc="m_sx2", k=16)
            sx2t = p.tok("m_sx2")
            if pre == "combine":
                yx2 = p.y(lambda y, tt=tt: y.dma_start(out=yt[:, :], in_=dram["y_in"][128 * tt:128 * (tt + 1), :]),
                          after=[sx2t], inc="m_sx2", k=16)
                sx2t = p.tok("m_sx2")
                v0 = p.v(lambda v: v.tensor_tensor(yt[:, :], stile[:, :], yt[:, :], ALU.add),
                         after=[sx2t], inc="v")
                if "y_in2" in dram:
                    sxb2 = p.y(lambda y, tt=tt: y.dma_start(out=stile[:, :], in_=dram["y_in2"][128 * tt:128 * (tt + 1), :]),
                               after=[v0], inc="m_sx2", k=16)
                    sxb2 = p.tok("m_sx2")
                    v0 = p.v(lambda v: v.tensor_tensor(yt[:, :], stile[:, :], yt[:, :], ALU.add),
                             after=[sxb2], inc="v")
                lnt, _ = emit_ln(p, b, yt[:, :], stile[:, :], [v0], (tiny, 8, []), eps_ap=epsc)
                sx2t = lnt[0]
            t_in, trel1 = emit_ln(p, b, acc_tile, lt2[:, :], [acc_done, sx2t] + tg, (tiny, 16 + 8 * jt, []),
                                  gamma=lnw[:, :], beta=lnb[:, :], eps_ap=eps12, resid=stile[:, :])
            t_fin, trel2 = emit_ln(p, b, lt2[:, :], acc_tile, t_in, (tiny, 16 + 8 * jt, [trel1]),
                                   eps_ap=epsc)
        else:
            t_fin, trel2 = emit_ln(p, b, acc_tile, lt2[:, :], [acc_done] + tg, (tiny, 16 + 8 * jt, []),
                                   eps_ap=epsc)
        ti2.release(jt, trel2)
        src_out = acc_tile if inner_ln else lt2[:, :]
        so_ = p.y(lambda y, tt=tt, src_out=src_out: y.dma_start(
            out=dram["s_out"][128 * tt:128 * (tt + 1), :], in_=src_out),
            after=t_fin, inc=f"m_so{tt % 2}", k=16)
        if inner_ln:
            st2.release(0, so_)
            lt_rot.release(0, t_fin[0])
        else:
            lt_rot.release(0, so_)
    p.y(lambda y: y.nop(), after=["m_so0", "m_so1"], inc="m_fin")


def moe_sbufs(inner_ln, pre):
    out = [
    ("fA", [128, 16384], F32),
    ("m_s1T", [128, 16384], BF16),
    ("m_s1Tf", [128, 1024], F32),
    ("m_hq", [128, 16384], BF16),
    ("m_w1a", [128, 8192], BF16), ("m_w1b", [128, 8192], BF16),
    ("m_w2", [128, 8192], BF16),
    ("m_wrep", [128, 2048], BF16),
    ("m_st", [128, 1024], F32),
    ("m_wr", [128, 64], F32), ("m_br", [1, 8], F32),
    ("m_b1r", [128, 256], F32), ("m_b2s", [8, 1024], BF16),
    ("m_sel8", [8, 1024], BF16),
    ("m_lt", [128, 8], F32), ("m_tops", [128, 8], F32), ("m_tidx", [128, 8], U32),
    ("m_wtm", [128, 128], F32), ("m_w8", [8, 2048], BF16),
    ("m_eye", [128, 128], F32), ("m_ones1", [1, 128], F32),
    ("m_yt", [128, 1024], F32),
    ("m_tiny", [128, 72], F32),
    ]
    if inner_ln:
        out += [("m_lnw", [128, 1024], F32), ("m_lnb", [128, 1024], F32)]
        out = [x for x in out if x[0] != "m_w1b"]
    if pre != "combine":
        out = [x for x in out if x[0] != "m_yt"]
    return out
MOE_PSUMS = [(f"mpp{i}", [128, 512], F32) for i in range(6)]


def moe_dram_shapes(inner_ln, pre):
    d = {
        "s_in": [T_OWN, D], "W1": [8, D, 4096], "b1r": [128, 256],
        "W2": [8, 4096, D], "b2s": [8, 1024], "Wr": [D, 8], "brs": [1, 8],
        "sel8": [8, 1024], "eye": [128, 128],
    }
    if inner_ln:
        d["lnw_rep"] = [128, 1024]
        d["lnb_rep"] = [128, 1024]
    if pre == "combine":
        d["y_in"] = [T_OWN, D]
        d["y_in2"] = [T_OWN, D]
    return d


def build_moe_standalone(inner_ln=False, pre="none", two_y=True):
    nc = bass.Bass("TRN2", target_bir_lowering=False)
    shapes = moe_dram_shapes(inner_ln, pre)
    if pre == "combine" and not two_y:
        shapes.pop("y_in2")
    dram = {nm: nc.dram_tensor(nm, sh, F32, kind="ExternalInput").ap() for nm, sh in shapes.items()}
    dram["s_out"] = nc.dram_tensor("s_out", [T_OWN, D], F32, kind="ExternalOutput").ap()
    return build_prog(nc, [lambda p, b: emit_moe(p, b, dram, pre, inner_ln)],
                      moe_sbufs(inner_ln, pre), MOE_PSUMS)


def stage_moe_inputs(layer, inputs, s_in, y_in=None):
    if layer == 0:
        W1 = inputs["L0_ffn_W1"]
        b1 = inputs["L0_ffn_b1"]
        W2 = inputs["L0_ffn_W2"]
        b2 = inputs["L0_ffn_b2"]
        Wr = inputs["L0_ffn_Wr"]
        br = inputs["L0_ffn_br"]
    else:
        W1 = inputs["L1_ffn_W1"].transpose(0, 2, 1)
        b1 = inputs["L1_ffn_b1"]
        W2 = inputs["L1_ffn_W2"].transpose(0, 2, 1)
        b2 = inputs["L1_ffn_b2"]
        Wr = inputs["L1_ffn_Wr"].T
        br = inputs["L1_ffn_br"]
    sel8 = np.zeros((8, 1024), np.float32)
    for e in range(8):
        sel8[e, 128 * e:128 * (e + 1)] = 1.0
    d = {
        "s_in": s_in,
        "W1": np.ascontiguousarray(W1), "W2": np.ascontiguousarray(W2),
        "b1r": np.ascontiguousarray(np.concatenate(
            [b1[e].reshape(32, 128).T for e in range(8)], axis=1)),
        "b2s": np.ascontiguousarray(b2),
        "Wr": np.ascontiguousarray(Wr), "brs": br.reshape(1, 8),
        "sel8": sel8, "eye": np.eye(128, dtype=np.float32),
    }
    if layer == 1:
        d["lnw_rep"] = np.tile(inputs["L1_ffn_ln_w"][None, :], (128, 1))
        d["lnb_rep"] = np.tile(inputs["L1_ffn_ln_b"][None, :], (128, 1))
    if y_in is not None:
        d["y_in"] = y_in
    return {k: np.ascontiguousarray(np.asarray(v, np.float32)) for k, v in d.items()}



TC = 512
NCH = 8

def emit_mamba(p, b, dram, dbg=0):
    DI = 2048
    N16 = 16
    fwT = b["b_inwT"]
    owT = b["b_owT"]
    xpT = b["b_xpT"]
    dwT = b["b_dwT"]
    xi = b["b_xi"]
    za = b["b_za"]
    xc = b["b_xc"]
    convw = b["b_convw"]
    convb = b["b_convb"]
    dtb16 = b["b_dtb16"]
    A256 = b["b_Alog256"]
    D16 = b["b_D16"]
    state = b["b_state"]
    Rv = b["b_Rv"]
    G8 = b["b_G8"]
    R16 = b["b_R16"]
    lng = b["b_lng"]
    lnb = b["b_lnb"]
    eyef = b["b_eye"]
    stile = b["b_st"]
    lnx = b["b_lnx"]
    xnT = b["b_xnT"]
    dtbuf = b["b_dtb"]
    Bb = b["b_Bb"]
    Cb = b["b_Cb"]
    Brep = b["b_Brep"]
    Crep = b["b_Crep"]
    deltaG = b["b_dG"]
    duG = b["b_duG"]
    dAb = b["b_dA"]
    dBub = b["b_dBu"]
    hb = b["b_h"]
    hCb = b["b_hC"]
    ytile = b["b_yt"]
    yev = b["b_yev"]
    sps = b["b_sps"]
    tiny = b["b_tiny"]
    pool = PsumPool(p, [b[f"bpp{i}"] for i in range(4)])
    psy = [b["bpy0"], b["bpy1"]]

    ld, ldg = "b_ld", "b_ldg"
    p.g(lambda g: g.dma_start(out=owT[:, :].rearrange("p (c m) -> p c m", c=16),
                              in_=dram["out_wT"].rearrange("(c p) m -> p c m", p=128)), inc=ldg, k=16)
    p.g(lambda g: g.dma_start(out=xpT[:, :].rearrange("p (c m) -> p c m", c=16),
                              in_=dram["xprojT"].rearrange("(c p) m -> p c m", p=128)), inc=ldg, k=16)
    p.g(lambda g: g.dma_start(out=dwT[:, :].rearrange("p (c m) -> p c m", c=16),
                              in_=dram["dt_wT"].rearrange("(c p) m -> p c m", p=64)), inc=ldg, k=16)
    p.g(lambda g: g.dma_start(out=Rv[:, :], in_=dram["Rv"]), inc=ldg, k=16)
    p.g(lambda g: g.dma_start(out=G8[:, :], in_=dram["G8"]), inc=ldg, k=16)
    p.g(lambda g: g.dma_start(out=R16[:, :], in_=dram["R16"]), inc=ldg, k=16)
    for nm, dst in [("conv_w4", convw), ("conv_b16", convb), ("dt_b16", dtb16),
                    ("Alog256", A256), ("D_16", D16), ("eye", eyef),
                    ("lng_rep", lng), ("lnb_rep", lnb)]:
        p.y(lambda y, nm=nm, dst=dst: y.dma_start(out=dst[:, :], in_=dram[nm]), inc=ld, k=16)
    ld_tok, ldg_tok = p.tok(ld), p.tok(ldg)
    epsc = tiny[:, 64:65]
    onec = tiny[:, 66:67]
    p.v(lambda v: v.memset(epsc, LN_EPS), inc="v")
    p.v(lambda v: v.memset(onec, 1.0), inc="v")
    sA = p.s(lambda s: s.activation(A256[:, :], A256[:, :], AF.Exp), after=[ld_tok], inc="s")
    vA = p.v(lambda v: v.tensor_scalar(A256[:, :], A256[:, :], -1.0, None, ALU.mult),
             after=[sA], inc="v")
    for cc in range(16):
        p.v(lambda v, cc=cc: v.memset(xi[:, 516 * cc:516 * cc + 4], 0.0), inc="v")
    init_tok = p.tok("v")

    st_rot = RotBuf(1)
    ti_rot = RotBuf(2)
    lnx_rot = RotBuf(1)
    fw_rot = RotBuf(1)
    xnT_rot = RotBuf(1)
    dt_rot = RotBuf(2)
    bc_rot = RotBuf(2)
    rep_rot = RotBuf(2)
    dG_rot = RotBuf(3)
    duG_rot = RotBuf(2)
    dA_rot = RotBuf(2)
    dBu_rot = RotBuf(2)
    h_rot = RotBuf(2)
    hC_rot = RotBuf(2)
    psy_rot = RotBuf(2)
    yev_rot = RotBuf(2)
    yt_rot = RotBuf(1)
    state_tok = {}
    xi_guard = None
    conv_prev = None
    prev_dv = None
    yg_war = None

    dbg_toks = {}
    for c in range(1 if dbg else NCH):
        _, xng = xnT_rot.get()
        ln_toks = []
        for ts_ in range(4):
            jr, sg = st_rot.get()
            stl = stile[:, 0:1024]
            sx = p.y(lambda y, c=c, ts_=ts_, stl=stl: y.dma_start(
                out=stl, in_=dram["xseq"][TC * c + 128 * ts_: TC * c + 128 * (ts_ + 1), :]),
                after=sg, inc=f"b_sx{jr}", k=16)
            sxt = p.tok(f"b_sx{jr}")
            jt, tg = ti_rot.get()
            jlx, lxg = lnx_rot.get()
            lnx_c = lnx[:, 1024 * 0:1024 * 1]
            lnt, trel = emit_ln(p, b, stl, lnx_c, [sxt, init_tok] + tg + lxg, (tiny, 8 * jt, []),
                                gamma=lng[:, :], beta=lnb[:, :], eps_ap=epsc)
            ti_rot.release(jt, lnt[0])
            for dc in range(8):
                j, ps, guard = pool.get()
                trt = p.t(lambda t, ps=ps, lnx_c=lnx_c, dc=dc: t.transpose(
                    ps[:, 0:128], lnx_c[:, 128 * dc:128 * (dc + 1)], eyef[:, :]),
                    after=lnt + guard + xng + [ld_tok], inc="t")
                ev = p.s(lambda s, ps=ps, dc=dc, ts_=ts_: s.copy(
                    xnT[:, 512 * dc + 128 * ts_: 512 * dc + 128 * (ts_ + 1)], ps[:, 0:128]),
                    after=[trt], inc="s")
                pool.release(j, ev)
                ln_toks.append(ev)
            st_rot.release(jr, lnt[0])
            lnx_rot.release(jlx, ln_toks[-1])
            if dbg == 3 and c == 0 and ts_ == 0:
                p.g(lambda g: g.dma_start(out=dram["y_out"][0:128, :], in_=lnx[:, :]),
                    after=[p.tok("s"), p.tok("v")], inc="b_dump", k=16)
                p.g(lambda g: g.dma_start(
                    out=dram["y_out"][128:256, 0:512],
                    in_=xnT[:, 0:512]),
                    after=[p.tok("s")], inc="b_dump", k=16)
                p.g(lambda g: g.nop(), after=["b_dump"], inc="b_fin")
                return
        xn_done = p.tok("s")
        if dbg == 2 and c == 0:
            p.g(lambda g: g.dma_start(
                out=dram["y_out"][0:512, :].rearrange("(a p) d -> p a d", p=128),
                in_=xnT[:, :].rearrange("p (a d) -> p a d", a=4)),
                after=[xn_done], inc="b_dump", k=16)
            p.g(lambda g: g.nop(), after=["b_dump"], inc="b_fin")
            return

        prev_xi_guard = xi_guard
        for half in range(2):
            jw, fwg = fw_rot.get()
            fw_c = fwT[:, :]
            p.g(lambda g, half=half, fw_c=fw_c: g.dma_start(
                out=fw_c.rearrange("p (c m) -> p c m", c=8),
                in_=dram["in_wT"].rearrange("(c p) m -> p c m", p=128)[:, :, 2048 * half:2048 * (half + 1)]),
                after=fwg, inc=f"b_fw{jw}", k=16)
            fw_tok = p.tok(f"b_fw{jw}")
            last_mm = None
            for mch0 in range(16):
                mch = 16 * half + mch0
                j, ps, guard = pool.get()
                for kc in range(8):
                    p.t(lambda t, ps=ps, mch0=mch0, kc=kc, fw_c=fw_c: t.matmul(
                        ps[:, 0:512],
                        fw_c[:, 2048 * kc + 128 * mch0: 2048 * kc + 128 * (mch0 + 1)],
                        xnT[:, 512 * kc:512 * (kc + 1)],
                        start=(kc == 0), stop=(kc == 7)),
                        after=([xn_done, ldg_tok, fw_tok] + guard +
                               ([prev_xi_guard] if (prev_xi_guard and mch == 0) else [])) if kc == 0 else None,
                        inc=("t" if kc == 7 else None))
                mm_tok = p.tok("t")
                if mch < 16:
                    ev = p.s(lambda s, ps=ps, mch=mch: s.copy(
                        xi[:, 516 * mch + 4: 516 * mch + 516], ps[:, 0:512]),
                        after=[mm_tok], inc="s")
                else:
                    ev = p.s(lambda s, ps=ps, mch=mch: s.activation(
                        za[:, 512 * (mch - 16): 512 * (mch - 15)], ps[:, 0:512], AF.Silu),
                        after=[mm_tok], inc="s")
                pool.release(j, ev)
            fw_rot.release(jw, p.tok("t"))
        xz_done = p.tok("s")

        conv_toks = []
        for cc in range(16):
            c0 = p.v(lambda v, cc=cc: v.tensor_scalar(
                sps[:, 0:512], xi[:, 516 * cc + 1: 516 * cc + 513], convw[:, 4 * cc:4 * cc + 1],
                None, ALU.mult),
                after=[xz_done, init_tok] + (conv_toks[-1:] if conv_toks else ([conv_prev] if conv_prev else [])),
                inc="v")
            for k in range(1, 4):
                c0 = p.v(lambda v, cc=cc, k=k: v.scalar_tensor_tensor(
                    sps[:, 0:512], xi[:, 516 * cc + 1 + k: 516 * cc + 513 + k],
                    convw[:, 4 * cc + k:4 * cc + k + 1], sps[:, 0:512], ALU.mult, ALU.add),
                    after=[c0], inc="v")
            sc_ = p.s(lambda s, cc=cc: s.activation(
                xc[:, 512 * cc:512 * (cc + 1)], sps[:, 0:512], AF.Silu, bias=convb[:, cc:cc + 1]),
                after=[c0], inc="s")
            conv_toks.append(sc_)
        conv_prev = conv_toks[-1]
        xc_done = p.tok("s")
        halo_toks = []
        for cc in range(16):
            ht = p.v(lambda v, cc=cc: v.tensor_copy(
                xi[:, 516 * cc + 1:516 * cc + 4], xi[:, 516 * cc + 513:516 * cc + 516]),
                after=[xc_done], inc="v")
            halo_toks.append(ht)
        xi_guard = halo_toks[-1]

        jdt, dtg = dt_rot.get()
        dtb_c = dtbuf[:, 512 * jdt:512 * (jdt + 1)]
        jbc, bcg = bc_rot.get()
        Bb_c = Bb[:, 512 * jbc:512 * (jbc + 1)]
        Cb_c = Cb[:, 512 * jbc:512 * (jbc + 1)]
        for (cols, outap, parts) in [((0, 64), dtb_c, 64), ((64, 80), Bb_c, 16), ((80, 96), Cb_c, 16)]:
            j, ps, guard = pool.get()
            for kc in range(16):
                p.t(lambda t, ps=ps, kc=kc, cols=cols, parts=parts: t.matmul(
                    ps[0:parts, 0:512],
                    xpT[:, 96 * kc + cols[0]: 96 * kc + cols[1]],
                    xc[:, 512 * kc:512 * (kc + 1)],
                    start=(kc == 0), stop=(kc == 15)),
                    after=([xc_done] + guard + dtg + bcg) if kc == 0 else None,
                    inc=("t" if kc == 15 else None))
            mm_tok = p.tok("t")
            ev = p.s(lambda s, ps=ps, outap=outap, parts=parts: s.copy(outap, ps[0:parts, 0:512]),
                     after=[mm_tok], inc="s")
            pool.release(j, ev)
        xp_done = p.tok("s")

        jrep, repg = rep_rot.get()
        Brep_c = Brep[:, 512 * jrep:512 * (jrep + 1)]
        Crep_c = Crep[:, 512 * jrep:512 * (jrep + 1)]
        for (srcap, dstap) in [(Bb_c, Brep_c), (Cb_c, Crep_c)]:
            j, ps, guard = pool.get()
            mmr = p.t(lambda t, ps=ps, srcap=srcap: t.matmul(
                ps[:, 0:512], R16[:, :], srcap, start=True, stop=True),
                after=[xp_done, ldg_tok] + guard + repg, inc="t")
            ev = p.s(lambda s, ps=ps, dstap=dstap: s.copy(dstap, ps[:, 0:512]),
                     after=[mmr], inc="s")
            pool.release(j, ev)
        rep_done = p.tok("s")
        if dbg and c == 0:
            dbg_toks["dtb"] = (dtb_c, Bb_c, Cb_c, Brep_c, Crep_c, [xp_done, rep_done])

        if dbg and c == 0:
            dbg_toks["xc"] = [xc_done]
            dbg_toks["xp"] = None
        for G in range(16):
            jG, dGg = dG_rot.get()
            dG_c = deltaG[:, 512 * jG:512 * (jG + 1)]
            j, ps, guard = pool.get()
            dmm = p.t(lambda t, ps=ps, G=G: t.matmul(
                ps[:, 0:512], dwT[:, 128 * G:128 * (G + 1)], dtb_c,
                start=True, stop=True),
                after=[xp_done] + guard, inc="t")
            r_ = p.s(lambda s, ps=ps, G=G: s.activation(sps[:, 512:1024], ps[:, 0:512], AF.Relu,
                                                        bias=dtb16[:, G:G + 1]),
                     after=[dmm] + ([prev_dv] if prev_dv else []), inc="s")
            a_ = p.s(lambda s, ps=ps, G=G: s.activation(sps[:, 1024:1536], ps[:, 0:512], AF.Abs,
                                                        bias=dtb16[:, G:G + 1]),
                     after=[dmm, r_], inc="s")
            e_ = p.s(lambda s: s.activation(sps[:, 1024:1536], sps[:, 1024:1536], AF.Exp, scale=-1.0),
                     after=[a_], inc="s")
            l_ = p.s(lambda s: s.activation(sps[:, 1024:1536], sps[:, 1024:1536], AF.Ln, bias=onec),
                     after=[e_, init_tok], inc="s")
            dv = p.v(lambda v, dG_c=dG_c: v.tensor_tensor(dG_c, sps[:, 512:1024], sps[:, 1024:1536], ALU.add),
                     after=[l_] + dGg, inc="v")
            prev_dv = dv
            pool.release(j, dv)
            jdu, duGg = duG_rot.get()
            duG_c = duG[:, 512 * jdu:512 * (jdu + 1)]
            duv = p.v(lambda v, duG_c=duG_c, G=G, dG_c=dG_c: v.tensor_tensor(
                duG_c, dG_c, xc[:, 512 * G:512 * (G + 1)], ALU.mult),
                after=[dv] + duGg, inc="v")
            for m in range(16):
                i = 16 * G + m
                j1, ps1, g1 = pool.get()
                mm1 = p.t(lambda t, ps=ps1, m=m, dG_c=dG_c: t.matmul(
                    ps[:, 0:512], Rv[:, 128 * m:128 * (m + 1)], dG_c, start=True, stop=True),
                    after=[dv] + g1, inc="t")
                j2, ps2, g2 = pool.get()
                mm2 = p.t(lambda t, ps=ps2, m=m, duG_c=duG_c: t.matmul(
                    ps[:, 0:512], Rv[:, 128 * m:128 * (m + 1)], duG_c, start=True, stop=True),
                    after=[duv] + g2, inc="t")
                jA, gA = dA_rot.get()
                dA_c = dAb[:, 512 * jA:512 * (jA + 1)]
                adA = p.s(lambda s, ps=ps1, i=i, dA_c=dA_c: s.activation(
                    dA_c, ps[:, 0:512], AF.Exp, scale=A256[:, i:i + 1]),
                    after=[mm1, vA] + gA, inc="s")
                pool.release(j1, adA)
                jB, gB = dBu_rot.get()
                dBu_c = dBub[:, 512 * jB:512 * (jB + 1)]
                vBu = p.v(lambda v, ps=ps2, dBu_c=dBu_c, Brep_c=Brep_c: v.tensor_tensor(
                    dBu_c, ps[:, 0:512], Brep_c, ALU.mult),
                    after=[mm2, rep_done] + gB, inc="v")
                pool.release(j2, vBu)
                jh, gh = h_rot.get()
                h_c = hb[:, 512 * jh:512 * (jh + 1)]
                if c == 0:
                    sc_ = p.v(lambda v, h_c=h_c, dA_c=dA_c, dBu_c=dBu_c: v.tensor_tensor_scan(
                        h_c, dA_c, dBu_c, 0.0, ALU.mult, ALU.add),
                        after=[adA, vBu] + gh, inc="v")
                else:
                    sc_ = p.v(lambda v, h_c=h_c, dA_c=dA_c, dBu_c=dBu_c, i=i: v.tensor_tensor_scan(
                        h_c, dA_c, dBu_c, state[:, i:i + 1], ALU.mult, ALU.add),
                        after=[adA, vBu, state_tok[i]] + gh, inc="v")
                dA_rot.release(jA, sc_)
                dBu_rot.release(jB, sc_)
                stc = p.v(lambda v, h_c=h_c, i=i: v.tensor_copy(state[:, i:i + 1], h_c[:, 511:512]),
                          after=[sc_], inc="v")
                state_tok[i] = stc
                if dbg and c == 0 and i == 0:
                    dbg_toks["t0"] = (dA_c, dBu_c, h_c, [p.tok("v"), p.tok("s")])
                jC, gC = hC_rot.get()
                hC_c = hCb[:, 512 * jC:512 * (jC + 1)]
                vhC = p.v(lambda v, hC_c=hC_c, h_c=h_c, Crep_c=Crep_c: v.tensor_tensor(
                    hC_c, h_c, Crep_c, ALU.mult),
                    after=[sc_, rep_done] + gC, inc="v")
                h_rot.release(jh, vhC)
                if m % 8 == 0:
                    jy, gy = psy_rot.get()
                mmy = p.t(lambda t, jy=jy, m=m, hC_c=hC_c: t.matmul(
                    psy[jy][0:64, 0:512], G8[:, 64 * (m % 8): 64 * (m % 8 + 1)], hC_c,
                    start=(m % 8 == 0), stop=(m % 8 == 7)),
                    after=[vhC] + (gy if m % 8 == 0 else []), inc="t")
                hC_rot.release(jC, mmy)
                if m % 8 == 7:
                    kk = 2 * G + m // 8
                    cc2 = kk // 2
                    half = 64 * (kk % 2)
                    jyev, gyev = yev_rot.get()
                    ye_h = yev[half:half + 64, 512 * jyev:512 * (jyev + 1)]
                    e1 = p.v(lambda v, jy=jy, ye_h=ye_h, cc2=cc2, half=half: v.scalar_tensor_tensor(
                        ye_h, xc[half:half + 64, 512 * cc2:512 * (cc2 + 1)],
                        D16[half:half + 64, cc2:cc2 + 1], psy[jy][0:64, 0:512],
                        ALU.mult, ALU.add),
                        after=[mmy] + gyev, inc="v")
                    psy_rot.release(jy, e1)
                    e2 = p.v(lambda v, ye_h=ye_h, cc2=cc2, half=half: v.tensor_tensor(
                        b["b_yg"][half:half + 64, 512 * cc2:512 * (cc2 + 1)],
                        ye_h, za[half:half + 64, 512 * cc2:512 * (cc2 + 1)], ALU.mult),
                        after=[e1] + ([yg_war] if yg_war else []), inc="v")
                    yev_rot.release(jyev, e2)
            dG_rot.release(jG, p.tok("t"))
            duG_rot.release(jdu, p.tok("t"))
        dt_rot.release(jdt, p.tok("t"))
        bc_rot.release(jbc, p.tok("t"))
        rep_rot.release(jrep, p.tok("v"))
        yg_done = p.tok("v")
        if dbg and c == 0:
            yo = dram["y_out"]
            dA_c, dBu_c, h_c, t0toks = dbg_toks["t0"]
            dtb_d, Bb_d, Cb_d, Brep_d, Crep_d, dttoks = dbg_toks["dtb"]
            dumps = [
                (xc.rearrange("p (a d) -> p a d", a=8), yo[0:1024, :].rearrange("(a p) d -> p a d", p=128), dbg_toks["xc"]),
                (za.rearrange("p (a d) -> p a d", a=8), yo[1024:2048, :].rearrange("(a p) d -> p a d", p=128), dbg_toks["xc"]),
                (dtb_d, yo[2048:2112, 0:512], dttoks),
                (Bb_d, yo[2112:2128, 0:512], dttoks),
                (Cb_d, yo[2128:2144, 0:512], dttoks),
                (Brep_d, yo[2176:2304, 0:512], dttoks),
                (dA_c, yo[2304:2432, 0:512], t0toks),
                (dBu_c, yo[2432:2560, 0:512], t0toks),
                (h_c, yo[2560:2688, 0:512], t0toks),
                (yg.rearrange("p (a d) -> p a d", a=8) if False else b["b_yg"].rearrange("p (a d) -> p a d", a=8),
                 yo[3072:4096, :].rearrange("(a p) d -> p a d", p=128), [yg_done]),
            ]
            for (srcv, dstv, toks) in dumps:
                p.g(lambda g, srcv=srcv, dstv=dstv: g.dma_start(out=dstv, in_=srcv),
                    after=toks, inc="b_dump", k=16)
            p.g(lambda g: g.nop(), after=["b_dump"], inc="b_fin")
            return

        yg = b["b_yg"]
        for ts_ in range(4):
            jy2, ytg = yt_rot.get()
            yt_c = ytile[:, 0:1024]
            evs = []
            for dn in range(2):
                j, ps, guard = pool.get()
                for kc in range(16):
                    p.t(lambda t, ps=ps, ts_=ts_, dn=dn, kc=kc: t.matmul(
                        ps[:, 0:512],
                        yg[:, 512 * kc + 128 * ts_: 512 * kc + 128 * (ts_ + 1)],
                        owT[:, 1024 * kc + 512 * dn: 1024 * kc + 512 * (dn + 1)],
                        start=(kc == 0), stop=(kc == 15)),
                        after=([yg_done] + guard) if kc == 0 else None,
                        inc=("t" if kc == 15 else None))
                mm_tok = p.tok("t")
                ev = p.s(lambda s, ps=ps, yt_c=yt_c, dn=dn: s.activation(
                    yt_c[:, 512 * dn:512 * (dn + 1)], ps[:, 0:512], AF.Identity, scale=0.5),
                    after=[mm_tok] + (ytg if dn == 0 else []), inc="s")
                evs.append(ev)
                pool.release(j, ev)
            so_ = p.y(lambda y, c=c, ts_=ts_, yt_c=yt_c: y.dma_start(
                out=dram["y_out"][TC * c + 128 * ts_: TC * c + 128 * (ts_ + 1), :], in_=yt_c),
                after=evs, inc="b_so0", k=16)
            yt_rot.release(jy2, so_)
        yg_war = p.tok("t")
    p.y(lambda y: y.nop(), after=["b_so0"], inc="b_fin")


MAMBA_SBUFS = [
    ("b_inwT", [128, 16384], BF16),
    ("b_owT", [128, 16384], BF16),
    ("b_xpT", [128, 1536], BF16),
    ("b_dwT", [64, 2048], BF16),
    ("b_xi", [128, 16 * 516], BF16),
    ("b_za", [128, 8192], BF16),
    ("b_xc", [128, 8192], BF16),
    ("b_yg", [128, 8192], BF16),
    ("b_convw", [128, 64], F32), ("b_convb", [128, 16], F32),
    ("b_dtb16", [128, 16], F32), ("b_Alog256", [128, 256], F32),
    ("b_D16", [128, 16], F32), ("b_state", [128, 256], F32),
    ("b_Rv", [128, 2048], BF16), ("b_G8", [128, 512], BF16), ("b_R16", [16, 128], BF16),
    ("b_lng", [128, 1024], F32), ("b_lnb", [128, 1024], F32),
    ("b_eye", [128, 128], F32),
    ("b_st", [128, 1024], F32),
    ("b_lnx", [128, 1024], F32),
    ("b_xnT", [128, 4096], BF16),
    ("b_dtb", [64, 1024], BF16),
    ("b_Bb", [16, 1024], BF16), ("b_Cb", [16, 1024], BF16),
    ("b_Brep", [128, 1024], BF16), ("b_Crep", [128, 1024], BF16),
    ("b_dG", [128, 1536], BF16), ("b_duG", [128, 1024], BF16),
    ("b_dA", [128, 1024], BF16), ("b_dBu", [128, 1024], BF16),
    ("b_h", [128, 1024], F32), ("b_hC", [128, 1024], BF16),
    ("b_yt", [128, 1024], F32), ("b_yev", [128, 1024], F32),
    ("b_sps", [128, 1536], F32),
    ("b_tiny", [128, 72], F32),
]
MAMBA_PSUMS = [(f"bpp{i}", [128, 512], F32) for i in range(4)] + [
    ("bpy0", [64, 512], F32), ("bpy1", [64, 512], F32)]

def mamba_dram_shapes():
    return {
        "xseq": [S, D], "in_wT": [D, 2 * 2048], "out_wT": [2048, D],
        "xprojT": [2048, 96], "dt_wT": [64, 2048],
        "conv_w4": [128, 64], "conv_b16": [128, 16], "dt_b16": [128, 16],
        "Alog256": [128, 256], "D_16": [128, 16],
        "lng_rep": [128, 1024], "lnb_rep": [128, 1024],
        "Rv": [128, 2048], "G8": [128, 512], "R16": [16, 128], "eye": [128, 128],
    }


def build_mamba_standalone(dbg=0):
    nc = bass.Bass("TRN2", target_bir_lowering=False)
    dram = {nm: nc.dram_tensor(nm, sh, F32, kind="ExternalInput").ap()
            for nm, sh in mamba_dram_shapes().items()}
    dram["y_out"] = nc.dram_tensor("y_out", [S, D], F32, kind="ExternalOutput").ap()
    return build_prog(nc, [lambda p, b: emit_mamba(p, b, dram, dbg=dbg)], MAMBA_SBUFS, MAMBA_PSUMS)


def stage_mamba_inputs(core, inputs, xseq):
    pre = "L1_attn_f_" if core % 2 == 0 else "L1_attn_b_"
    DI = 2048
    conv_w = inputs[pre + "conv_w"]
    conv_w4 = np.zeros((128, 64), np.float32)
    for cc in range(16):
        conv_w4[:, 4 * cc:4 * (cc + 1)] = conv_w[128 * cc:128 * (cc + 1), :]
    Alog = inputs[pre + "A_log"]
    A256 = np.zeros((128, 256), np.float32)
    pp = np.arange(128)
    for i in range(256):
        A256[:, i] = Alog[8 * i + pp // 16, pp % 16]
    Rv = np.zeros((128, 2048), np.float32)
    for m in range(16):
        for cdst in range(128):
            Rv[8 * m + cdst // 16, 128 * m + cdst] = 1.0
    G8 = np.zeros((128, 512), np.float32)
    for j in range(8):
        for pp_ in range(128):
            G8[pp_, 64 * j + 8 * j + pp_ // 16] = 1.0
    R16 = np.zeros((16, 128), np.float32)
    for b_ in range(16):
        R16[b_, np.arange(128)[np.arange(128) % 16 == b_]] = 1.0
    d = {
        "xseq": xseq,
        "in_wT": inputs[pre + "in_w"].T,
        "out_wT": inputs[pre + "out_w"].T,
        "xprojT": inputs[pre + "xproj_w"].T,
        "dt_wT": inputs[pre + "dt_w"].T,
        "conv_w4": conv_w4,
        "conv_b16": inputs[pre + "conv_b"].reshape(16, 128).T,
        "dt_b16": inputs[pre + "dt_b"].reshape(16, 128).T,
        "Alog256": A256, "D_16": inputs[pre + "D_skip"].reshape(16, 128).T,
        "lng_rep": np.tile(inputs["L1_attn_ln_g"][None, :], (128, 1)),
        "lnb_rep": np.tile(inputs["L1_attn_ln_b"][None, :], (128, 1)),
        "Rv": Rv, "G8": G8, "R16": R16, "eye": np.eye(128, dtype=np.float32),
    }
    return {k: np.ascontiguousarray(np.asarray(v, np.float32)) for k, v in d.items()}



def kernel(**inputs):
    inputs = {k: np.asarray(v) for k, v in inputs.items()}
    cores = list(range(8))

    nc1 = build_l0_standalone()
    r1 = run_bass_kernel_spmd(nc1, [stage_l0_inputs(c, inputs) for c in cores], core_ids=cores)
    s1 = [r1.results[c]["s1_out"] for c in cores]

    nc2 = build_moe_standalone(inner_ln=False, pre="none")
    r2 = run_bass_kernel_spmd(nc2, [stage_moe_inputs(0, inputs, s1[c]) for c in cores], core_ids=cores)
    s2 = [r2.results[c]["s_out"] for c in cores]

    nc3 = build_mamba_standalone()
    in3 = []
    for c in cores:
        b_ = c // 2
        full = np.concatenate([s2[2 * b_], s2[2 * b_ + 1]], axis=0)
        xseq = full if c % 2 == 0 else np.ascontiguousarray(full[::-1])
        in3.append(stage_mamba_inputs(c, inputs, xseq))
    r3 = run_bass_kernel_spmd(nc3, in3, core_ids=cores)
    yh = [r3.results[c]["y_out"] for c in cores]

    nc4 = build_moe_standalone(inner_ln=True, pre="combine", two_y=True)
    in4 = []
    for c in cores:
        b_, h_ = c // 2, c % 2
        yf = yh[2 * b_]
        ybb = yh[2 * b_ + 1][::-1]
        sl = slice(2048 * h_, 2048 * (h_ + 1))
        im = stage_moe_inputs(1, inputs, s2[c], y_in=np.ascontiguousarray(yf[sl]))
        im["y_in2"] = np.ascontiguousarray(ybb[sl].astype(np.float32))
        in4.append(im)
    r4 = run_bass_kernel_spmd(nc4, in4, core_ids=cores)
    out = np.empty((NB, S, D), np.float32)
    for c in cores:
        b_, h_ = c // 2, c % 2
        out[b_, 2048 * h_:2048 * (h_ + 1)] = r4.results[c]["s_out"]
    return out
